# version 08
# baseline (speedup 1.0000x reference)
import numpy as np
from contextlib import ExitStack

import concourse.bass as bass
import concourse.mybir as mybir
from concourse.bass_utils import run_bass_kernel_spmd

F32 = mybir.dt.float32
AF = mybir.ActivationFunctionType
OP = mybir.AluOpType

B_LOC = 2
S = 2048
D = 1024
TOK = 256
NTILE = B_LOC * S // TOK
ENGS = ("sync", "gpsimd", "vector", "scalar", "tensor")


def build_nc():
    nc = bass.Bass("TRN2", target_bir_lowering=False)
    x_h = nc.dram_tensor("x", [B_LOC, S, D], F32, kind="ExternalInput").ap()
    o_h = nc.dram_tensor("o", [B_LOC, S, D], F32, kind="ExternalOutput").ap()
    winT_h = nc.dram_tensor("winT", [128, 8, 256], F32, kind="ExternalInput").ap()
    woutT_h = nc.dram_tensor("woutT", [128, 2, 1024], F32, kind="ExternalInput").ap()
    wskipT_h = nc.dram_tensor("wskipT", [128, 8, 1024], F32, kind="ExternalInput").ap()
    u1_h = nc.dram_tensor("u1", [128, 8, 256], F32, kind="ExternalInput").ap()
    v1_h = nc.dram_tensor("v1", [128, 2, 4096], F32, kind="ExternalInput").ap()
    u2_h = nc.dram_tensor("u2", [128, 32, 256], F32, kind="ExternalInput").ap()
    v2_h = nc.dram_tensor("v2", [128, 2, 1024], F32, kind="ExternalInput").ap()
    nwf_h = nc.dram_tensor("nwf", [128, 8], F32, kind="ExternalInput").ap()
    bo_h = nc.dram_tensor("bo", [128, 1024], F32, kind="ExternalInput").ap()
    bs_h = nc.dram_tensor("bs", [128, 1024], F32, kind="ExternalInput").ap()
    b2_h = nc.dram_tensor("b2", [128, 1024], F32, kind="ExternalInput").ap()
    bin_h = nc.dram_tensor("bin", [128, 2], F32, kind="ExternalInput").ap()
    b1_h = nc.dram_tensor("b1", [128, 32], F32, kind="ExternalInput").ap()
    ldec_h = nc.dram_tensor("ldec", [128, 2], F32, kind="ExternalInput").ap()
    eye_h = nc.dram_tensor("eye", [128, 128], F32, kind="ExternalInput").ap()

    es = ExitStack()
    sb = lambda name, shape: es.enter_context(nc.sbuf_tensor(name, shape, F32))
    WIN = sb("WIN", [128, 8, 256])
    WOUT = sb("WOUT", [128, 2, 1024])
    WSKIP = sb("WSKIP", [128, 8, 1024])
    V1 = sb("V1", [128, 2, 4096])
    U2 = sb("U2", [128, 32, 256])
    V2 = sb("V2", [128, 2, 1024])
    BADD = sb("BADD", [128, 1024])
    B2R = sb("B2R", [128, 1024])
    BIN = sb("BIN", [128, 2])
    B1 = sb("B1", [128, 32])
    LDEC = sb("LDEC", [128, 2])
    NWF = sb("NWF", [128, 8])
    DECS = sb("DECS", [128, 2])
    EYE = sb("EYE", [128, 128])
    U = sb("U", [128, 2, 2 * S])
    XA = sb("XA", [128, 2048])
    SCR = sb("SCR", [128, 2048])
    T0 = sb("T0", [128, 2048])
    ATT = sb("ATT", [128, 2048])
    FFNB = sb("FFNB", [128, 2048])
    XNT = sb("XNT", [128, 4, 256])
    ST6 = sb("ST6", [128, 4, 6])
    ST2 = sb("ST2", [128, 2, 2])
    MSQ = sb("MSQ", [128, 2])
    MS = sb("MS", [128, 2])
    SD = sb("SD", [128, 2])
    RR = sb("RR", [128, 2])
    ZERO = sb("ZERO", [128, 1])
    EPS5 = sb("EPS5", [128, 1])
    EPS4 = sb("EPS4", [128, 1])
    PADT = sb("PADT", [128, 2])
    PADS = sb("PADS", [128, 2])

    ps = lambda name, shape: es.enter_context(nc.psum_tensor(name, shape, F32))
    PT0 = ps("PT0", [128, 4, 128])
    PT1 = ps("PT1", [128, 4, 128])
    PA = ps("PA", [128, 512])
    PB = ps("PB", [128, 512])
    PU = ps("PU", [128, 512])
    PP = ps("PP", [128, 512])
    PH = ps("PH", [128, 2, 256])
    PQ = ps("PQ", [128, 2, 256])

    sem_names = ("LX", "LW", "LW2", "VS", "SS", "TS", "HS", "OS")
    semh = {s: es.enter_context(nc.semaphore(s)) for s in sem_names}

    ops = {e: [] for e in ENGS}
    cnt = {s: 0 for s in sem_names}

    def w(eng, sem, val):
        if val > 0:
            ops[eng].append(lambda eh, sm=sem, v=val: eh.wait_ge(semh[sm], v))

    def e(eng, fn, inc=None, k=1):
        if inc is None:
            ops[eng].append(fn)
            return None
        cnt[inc] += k
        ops[eng].append(lambda eh, f=fn, s=inc, kk=k: f(eh).then_inc(semh[s], kk))
        return cnt[inc]

    def vpad(inc=None):
        e("vector", lambda eh: eh.tensor_copy(PADT[:, 0:1], ZERO[:]))
        e("vector", lambda eh: eh.tensor_copy(PADT[:, 1:2], ZERO[:]))
        return e("vector", lambda eh: eh.tensor_copy(PADT[:, 0:1], ZERO[:]), inc=inc)

    def spad(inc=None):
        e("scalar", lambda eh: eh.activation(PADS[:, 0:1], ZERO[:], AF.Copy))
        e("scalar", lambda eh: eh.activation(PADS[:, 1:2], ZERO[:], AF.Copy))
        return e("scalar", lambda eh: eh.activation(PADS[:, 0:1], ZERO[:], AF.Copy), inc=inc)

    wcnt = {}
    for name, dst, src in (
        ("eye", EYE[:], eye_h[:]), ("nw", NWF[:], nwf_h[:]), ("bin", BIN[:], bin_h[:]),
        ("b1", B1[:], b1_h[:]), ("ldec", LDEC[:], ldec_h[:]), ("bo", BADD[:], bo_h[:]),
        ("bs", SCR[:, 0:1024], bs_h[:]), ("b2", B2R[:], b2_h[:]),
        ("winT", WIN[:], winT_h[:]), ("woutT", WOUT[:], woutT_h[:]),
        ("wskipT", WSKIP[:], wskipT_h[:]),
    ):
        wcnt[name] = e("sync", lambda eh, d=dst, s=src: eh.dma_start(out=d, in_=s), inc="LW", k=16)
    w2cnt = {}
    for name, dst, src in (("v1", V1[:], v1_h[:]), ("u2", U2[:], u2_h[:]),
                           ("v2", V2[:], v2_h[:])):
        w2cnt[name] = e("scalar", lambda eh, d=dst, s=src: eh.dma_start(out=d, in_=s), inc="LW2", k=16)

    w("scalar", "LW", wcnt["ldec"])
    e("scalar", lambda eh: eh.activation(DECS[:], LDEC[:], AF.Sigmoid))
    ss_decs = spad(inc="SS")

    e("vector", lambda eh: eh.memset(ZERO[:], 0.0))
    e("vector", lambda eh: eh.memset(EPS5[:], 1e-5))
    e("vector", lambda eh: eh.memset(EPS4[:], 1e-4))
    e("vector", lambda eh: eh.memset(PADT[:], 0.0))
    w("vector", "LW", wcnt["bs"])
    e("vector", lambda eh: eh.tensor_tensor(BADD[:], BADD[:], SCR[:, 0:1024], OP.add))
    vs_badd = vpad(inc="VS")
    w("vector", "LW", wcnt["winT"])
    for c in range(8):
        e("vector", lambda eh, cc=c: eh.tensor_scalar_mul(
            WIN[:, cc, :], WIN[:, cc, :], NWF[:, cc:cc + 1]))
    vs_winfold = vpad(inc="VS")

    XB = [XA, SCR]
    xfree = [0] * (2 * NTILE + 2)

    def emit_xload(slot, vs_guard):
        i = slot % NTILE
        b, t = i // 8, (i % 8) * TOK
        buf = XB[slot % 2]
        if vs_guard:
            w("gpsimd", "VS", vs_guard)
        for blk in range(2):
            e("gpsimd", lambda eh, bb=b, tt=t + blk * 128, dst=buf[:, blk * 1024:(blk + 1) * 1024]:
              eh.dma_start(out=dst, in_=x_h[bb, tt:tt + 128, :]), inc="LX", k=16)

    def ln_stats(src):
        for blk in range(2):
            for g in range(2):
                e("vector", lambda eh, b=blk, gg=g, s=src:
                  eh.bn_stats(ST6[:, 2 * b + gg, :],
                              s[:, b * 1024 + gg * 512: b * 1024 + (gg + 1) * 512]))
        for blk in range(2):
            e("vector", lambda eh, b=blk: eh.bn_aggr(ST2[:, b, :], ST6[:, 2 * b:2 * b + 2, :]))

    def sqrt_recip(src_col_fn, eps):
        for blk in range(2):
            e("scalar", lambda eh, b=blk: eh.activation(
                SD[:, b:b + 1], src_col_fn(b), AF.Sqrt, bias=eps[:, 0:1]))
        ss = spad(inc="SS")
        w("vector", "SS", ss)
        for blk in range(2):
            e("vector", lambda eh, b=blk: eh.reciprocal(RR[:, b:b + 1], SD[:, b:b + 1]))
        e("vector", lambda eh: eh.tensor_copy(PADT[:, 1:2], ZERO[:]))

    def emit_tp_group(src, g):
        for blk in range(2):
            PT = (PT0, PT1)[blk]
            for j in range(4):
                c = 4 * g + j
                e("tensor", lambda eh, bb=blk, cc=c, jj=j, P=PT, s=src: eh.transpose(
                    P[:, jj, :], s[:, bb * 1024 + cc * 128: bb * 1024 + (cc + 1) * 128], EYE[:]),
                  inc="TS" if j == 3 else None)
            w("vector", "TS", cnt["TS"])
            e("vector", lambda eh, bb=blk, P=PT: eh.tensor_copy(
                XNT[:, :, bb * 128:(bb + 1) * 128], P[:, :, :]))
            ev = vpad(inc="VS")
            w("tensor", "VS", ev)

    def vfront(slot, xn_dst, tp_guard_ts=0):
        buf = XB[slot % 2]
        w("vector", "LX", 32 * (slot + 1))
        vs_st = None
        ln_stats(buf)
        vs_st = vpad(inc="VS")
        w("scalar", "VS", vs_st)
        sqrt_recip(lambda blk: ST2[:, blk, 1:2], EPS5)
        for blk in range(2):
            e("vector", lambda eh, bb=blk, s=buf: eh.tensor_scalar(
                T0[:, bb * 1024:(bb + 1) * 1024], s[:, bb * 1024:(bb + 1) * 1024],
                ST2[:, bb, 0:1], RR[:, bb:bb + 1], OP.subtract, OP.mult))
        ln_stats(T0)
        for blk in range(2):
            e("vector", lambda eh, bb=blk: eh.tensor_tensor(
                MSQ[:, bb:bb + 1], ST2[:, bb, 0:1], ST2[:, bb, 0:1], OP.mult))
        for blk in range(2):
            e("vector", lambda eh, bb=blk: eh.tensor_tensor(
                MS[:, bb:bb + 1], MSQ[:, bb:bb + 1], ST2[:, bb, 1:2], OP.add))
        vs_st2 = vpad(inc="VS")
        w("scalar", "VS", vs_st2)
        sqrt_recip(lambda blk: MS[:, blk:blk + 1], EPS4)
        if tp_guard_ts:
            w("vector", "TS", tp_guard_ts)
        for blk in range(2):
            e("vector", lambda eh, bb=blk, d=xn_dst: eh.tensor_scalar_mul(
                d[:, bb * 1024:(bb + 1) * 1024], T0[:, bb * 1024:(bb + 1) * 1024],
                RR[:, bb:bb + 1]))
        return vpad(inc="VS")

    vs_uevac = [0] * NTILE
    vs_xn1 = [0] * NTILE
    ts_tp1 = [0] * NTILE

    emit_xload(0, 0)
    emit_xload(1, vs_badd)
    xnb = lambda i: (ATT, FFNB)[i % 2]
    vs_xn1[0] = vfront(0, xnb(0))
    xfree[0] = vs_xn1[0]
    emit_xload(2, xfree[0])
    vs_xn1[1] = vfront(1, xnb(1))
    xfree[1] = vs_xn1[1]
    emit_xload(3, xfree[1])

    for i in range(NTILE):
        b, t = i // 8, (i % 8) * TOK
        if i == 0:
            w("tensor", "VS", vs_winfold)
        w("tensor", "VS", vs_xn1[i])
        if i >= 1:
            w("tensor", "VS", vs_uevac[i - 1])
        for g in range(2):
            emit_tp_group(xnb(i), g)
            for nt in range(2):
                PO = (PU, PA)[nt]
                for c in range(4 * g, 4 * g + 4):
                    e("tensor", lambda eh, n=nt, cc=c, P=PO: eh.matmul(
                        P[:, 0:256], WIN[:, cc, n * 128:(n + 1) * 128],
                        XNT[:, cc % 4, :], start=(cc == 0), stop=(cc == 7)),
                      inc="TS" if (c == 7 and nt == 1) else None)
        ts_tp1[i] = cnt["TS"]
        w("vector", "TS", ts_tp1[i])
        if i == 0:
            w("vector", "LW", wcnt["bin"])
        for nt in range(2):
            PO = (PU, PA)[nt]
            e("vector", lambda eh, n=nt, bb=b, tt=t, P=PO: eh.tensor_scalar_add(
                U[:, n, bb * S + tt: bb * S + tt + TOK], P[:, 0:256],
                BIN[:, n:n + 1]))
        vs_uevac[i] = vpad(inc="VS")
        if i + 2 < NTILE:
            vs_xn1[i + 2] = vfront(i + 2, xnb(i + 2), tp_guard_ts=ts_tp1[i])
            xfree[i + 2] = vs_xn1[i + 2]
            if i + 4 < NTILE:
                emit_xload(i + 4, xfree[i + 2])

    w("vector", "LW", wcnt["wskipT"])
    for c in range(8):
        e("vector", lambda eh, cc=c: eh.tensor_scalar_mul(
            WSKIP[:, cc, :], WSKIP[:, cc, :], NWF[:, cc:cc + 1]))
    vpad()
    w("scalar", "TS", ts_tp1[NTILE - 1])
    w2cnt["u1"] = e("scalar", lambda eh: eh.dma_start(out=WIN[:], in_=u1_h[:]), inc="LW2", k=16)

    w("vector", "SS", ss_decs)
    for nt in range(2):
        e("vector", lambda eh: eh.memset(SCR[:], 1.0))
        vpad()
        e("vector", lambda eh, n=nt: eh.tensor_scalar_mul(SCR[:], SCR[:], DECS[:, n:n + 1]))
        vpad()
        for bb in range(2):
            e("vector", lambda eh, n=nt, b2=bb: eh.tensor_tensor_scan(
                U[:, n, b2 * S:(b2 + 1) * S], SCR[:], U[:, n, b2 * S:(b2 + 1) * S],
                ZERO[:, 0:1], OP.mult, OP.add))
    vs_scan = vpad(inc="VS")
    w("tensor", "VS", vs_scan)
    emit_xload(NTILE + 0, xfree[14])

    os_done = [0] * NTILE

    def vfront2(i):
        slot = NTILE + i
        buf = XB[i % 2]
        w("vector", "LX", 32 * (slot + 1))
        ln_stats(buf)
        vs_st = vpad(inc="VS")
        w("scalar", "VS", vs_st)
        sqrt_recip(lambda blk: ST2[:, blk, 1:2], EPS5)
        for blk in range(2):
            e("vector", lambda eh, bb=blk, s=buf: eh.tensor_scalar(
                T0[:, bb * 1024:(bb + 1) * 1024], s[:, bb * 1024:(bb + 1) * 1024],
                ST2[:, bb, 0:1], RR[:, bb:bb + 1], OP.subtract, OP.mult))
        ln_stats(T0)
        for blk in range(2):
            e("vector", lambda eh, bb=blk: eh.tensor_tensor(
                MSQ[:, bb:bb + 1], ST2[:, bb, 0:1], ST2[:, bb, 0:1], OP.mult))
        for blk in range(2):
            e("vector", lambda eh, bb=blk: eh.tensor_tensor(
                MS[:, bb:bb + 1], MSQ[:, bb:bb + 1], ST2[:, bb, 1:2], OP.add))
        vs_st2 = vpad(inc="VS")
        w("scalar", "VS", vs_st2)
        sqrt_recip(lambda blk: MS[:, blk:blk + 1], EPS4)
        for blk in range(2):
            e("vector", lambda eh, bb=blk: eh.tensor_scalar_mul(
                ATT[:, bb * 1024:(bb + 1) * 1024], T0[:, bb * 1024:(bb + 1) * 1024],
                RR[:, bb:bb + 1]))
        return vpad(inc="VS")

    vs_xn2 = [0] * NTILE
    vs_xn2[0] = vfront2(0)
    emit_xload(NTILE + 1, vs_scan)

    COMBO = [(0, 0, PA), (0, 1, PB), (1, 0, PU), (1, 1, PP)]
    for i in range(NTILE):
        b, t = i // 8, (i % 8) * TOK
        buf = XB[i % 2]
        if 1 <= i and i + 1 < NTILE:
            emit_xload(NTILE + i + 1, xfree[NTILE + i - 1])

        if i >= 1:
            vs_xn2[i] = vfront2(i)
        w("tensor", "VS", vs_xn2[i])
        for g in range(2):
            emit_tp_group(ATT, g)
            if g == 0:
                for blk, half, PO in COMBO:
                    for nt in range(2):
                        e("tensor", lambda eh, n=nt, bb=b, ttk=t + blk * 128, hh=half, P=PO: eh.matmul(
                            P[:], U[:, n, bb * S + ttk: bb * S + ttk + 128],
                            WOUT[:, n, hh * 512:(hh + 1) * 512],
                            start=(n == 0), stop=False))
            for blk, half, PO in COMBO:
                for c in range(4 * g, 4 * g + 4):
                    e("tensor", lambda eh, cc=c, bb=blk, hh=half, P=PO: eh.matmul(
                        P[:], XNT[:, cc % 4, bb * 128:(bb + 1) * 128],
                        WSKIP[:, cc, hh * 512:(hh + 1) * 512],
                        start=False, stop=(cc == 7)),
                      inc="TS" if (c == 7 and blk == 1 and half == 1) else None)
        ts_y = cnt["TS"]

        w("vector", "TS", ts_y)
        if i == 0:
            w("vector", "LW", wcnt["b2"])
        for blk, half, PO in COMBO:
            e("vector", lambda eh, bb=blk, hh=half, P=PO: eh.tensor_tensor(
                ATT[:, bb * 1024 + hh * 512: bb * 1024 + (hh + 1) * 512], P[:],
                T0[:, bb * 1024 + hh * 512: bb * 1024 + (hh + 1) * 512], OP.add))
        for blk, half, PO in COMBO:
            e("vector", lambda eh, bb=blk, hh=half: eh.tensor_tensor(
                ATT[:, bb * 1024 + hh * 512: bb * 1024 + (hh + 1) * 512],
                ATT[:, bb * 1024 + hh * 512: bb * 1024 + (hh + 1) * 512],
                BADD[:, hh * 512:(hh + 1) * 512], OP.add))
        ev = vpad(inc="VS")
        w("tensor", "VS", ev)
        ln_stats(ATT)
        vs_st = vpad(inc="VS")
        w("scalar", "VS", vs_st)
        sqrt_recip(lambda blk: ST2[:, blk, 1:2], EPS5)
        for blk in range(2):
            e("vector", lambda eh, bb=blk: eh.tensor_scalar(
                T0[:, bb * 1024:(bb + 1) * 1024], ATT[:, bb * 1024:(bb + 1) * 1024],
                ST2[:, bb, 0:1], RR[:, bb:bb + 1], OP.subtract, OP.mult))
        for blk in range(2):
            e("vector", lambda eh, bb=blk, s=buf: eh.tensor_tensor(
                s[:, bb * 1024:(bb + 1) * 1024], s[:, bb * 1024:(bb + 1) * 1024],
                T0[:, bb * 1024:(bb + 1) * 1024], OP.add))
        ln_stats(buf)
        vs_st = vpad(inc="VS")
        w("scalar", "VS", vs_st)
        sqrt_recip(lambda blk: ST2[:, blk, 1:2], EPS5)
        for blk in range(2):
            e("vector", lambda eh, bb=blk, s=buf: eh.tensor_scalar(
                T0[:, bb * 1024:(bb + 1) * 1024], s[:, bb * 1024:(bb + 1) * 1024],
                ST2[:, bb, 0:1], RR[:, bb:bb + 1], OP.subtract, OP.mult))
        vs_t1 = vpad(inc="VS")

        w("tensor", "VS", vs_t1)
        if i == 0:
            w("tensor", "LW2", w2cnt["u1"])
        for g in range(2):
            emit_tp_group(T0, g)
            for r in range(2):
                PO = (PU, PA)[r]
                for c in range(4 * g, 4 * g + 4):
                    e("tensor", lambda eh, rr=r, cc=c, P=PO: eh.matmul(
                        P[:, 0:256], WIN[:, cc, rr * 128:(rr + 1) * 128],
                        XNT[:, cc % 4, :], start=(cc == 0), stop=(cc == 7)),
                      inc="TS" if (c == 7 and r == 1) else None)
        w("vector", "TS", cnt["TS"])
        if i >= 1:
            w("vector", "OS", os_done[i - 1])
        for r in range(2):
            PO = (PU, PA)[r]
            e("vector", lambda eh, rr=r, P=PO: eh.tensor_copy(
                FFNB[:, rr * 256:(rr + 1) * 256], P[:, 0:256]))
        vs_p = vpad(inc="VS")
        w("tensor", "VS", vs_p)

        hs_base = cnt["HS"]
        for c in range(33):
            if c < 32:
                if c >= 2:
                    w("tensor", "HS", hs_base + c - 1)
                for r in range(2):
                    e("tensor", lambda eh, rr=r, cc=c: eh.matmul(
                        PH[:, cc % 2, :], V1[:, rr, cc * 128:(cc + 1) * 128],
                        FFNB[:, rr * 256:(rr + 1) * 256],
                        start=(rr == 0), stop=(rr == 1)),
                      inc="TS" if r == 1 else None)
                w("scalar", "TS", cnt["TS"])
                if i == 0 and c == 0:
                    w("scalar", "LW", wcnt["b1"])
                e("scalar", lambda eh, cc=c: eh.activation(
                    FFNB[:, 1536 + (cc % 2) * 256: 1536 + (cc % 2 + 1) * 256],
                    PH[:, cc % 2, :], AF.Gelu_apprx_tanh, bias=B1[:, cc:cc + 1]))
                spad(inc="HS")
            if c >= 1:
                cc0 = c - 1
                w("tensor", "HS", hs_base + cc0 + 1)
                for r in range(2):
                    PO = PQ[:, 0, :] if r == 0 else PB[:, 0:256]
                    e("tensor", lambda eh, rr=r, cc=cc0, P=PO: eh.matmul(
                        P, U2[:, cc, rr * 128:(rr + 1) * 128],
                        FFNB[:, 1536 + (cc % 2) * 256: 1536 + (cc % 2 + 1) * 256],
                        start=(cc == 0), stop=(cc == 31)),
                      inc="TS" if (r == 1 and cc0 == 31) else None)
        ts_q = cnt["TS"]

        w("vector", "TS", ts_q)
        for r in range(2):
            PO = PQ[:, 0, :] if r == 0 else PB[:, 0:256]
            e("vector", lambda eh, rr=r, P=PO: eh.tensor_copy(
                FFNB[:, 1024 + rr * 256:1024 + (rr + 1) * 256], P))
        vs_q = vpad(inc="VS")
        w("tensor", "VS", vs_q)
        for blk in range(2):
            for half in range(2):
                PO = (PA, PB)[half]
                for r in range(2):
                    e("tensor", lambda eh, rr=r, bb=blk, hh=half, P=PO: eh.matmul(
                        P[:], FFNB[:, 1024 + rr * 256 + bb * 128: 1024 + rr * 256 + (bb + 1) * 128],
                        V2[:, rr, hh * 512:(hh + 1) * 512],
                        start=(rr == 0), stop=(rr == 1)),
                      inc="TS" if r == 1 else None)
            w("vector", "TS", cnt["TS"])
            for half in range(2):
                PO = (PA, PB)[half]
                e("vector", lambda eh, bb=blk, hh=half, P=PO: eh.tensor_tensor(
                    FFNB[:, bb * 1024 + hh * 512: bb * 1024 + (hh + 1) * 512], P[:],
                    B2R[:, hh * 512:(hh + 1) * 512], OP.add))
            evb = vpad(inc="VS")
            w("tensor", "VS", evb)
        ln_stats(FFNB)
        vs_st = vpad(inc="VS")
        w("scalar", "VS", vs_st)
        sqrt_recip(lambda blk: ST2[:, blk, 1:2], EPS5)
        for blk in range(2):
            e("vector", lambda eh, bb=blk: eh.tensor_scalar(
                FFNB[:, bb * 1024:(bb + 1) * 1024], FFNB[:, bb * 1024:(bb + 1) * 1024],
                ST2[:, bb, 0:1], RR[:, bb:bb + 1], OP.subtract, OP.mult))
        for blk in range(2):
            e("vector", lambda eh, bb=blk, s=buf: eh.tensor_tensor(
                FFNB[:, bb * 1024:(bb + 1) * 1024], FFNB[:, bb * 1024:(bb + 1) * 1024],
                s[:, bb * 1024:(bb + 1) * 1024], OP.add))
        evf = vpad(inc="VS")
        xfree[NTILE + i] = evf
        w("gpsimd", "VS", evf)
        for blk in range(2):
            e("gpsimd", lambda eh, bb=b, ttk=t + blk * 128, o=blk: eh.dma_start(
                out=o_h[bb, ttk:ttk + 128, :], in_=FFNB[:, o * 1024:(o + 1) * 1024]),
              inc="OS", k=16)
        os_done[i] = cnt["OS"]

    w("gpsimd", "OS", cnt["OS"])

    with es, nc.Block() as block:
        @block.sync
        def _(sync):
            for f in ops["sync"]:
                f(sync)

        @block.gpsimd
        def _(gpsimd):
            for f in ops["gpsimd"]:
                f(gpsimd)

        @block.vector
        def _(vector):
            for f in ops["vector"]:
                f(vector)

        @block.scalar
        def _(scalar):
            for f in ops["scalar"]:
                f(scalar)

        @block.tensor
        def _(tensor):
            for f in ops["tensor"]:
                f(tensor)
    return nc


def kernel(L0_attn_tokens, L0_attn_norm_w, L0_attn_W_in, L0_attn_b_in,
           L0_attn_W_out, L0_attn_b_out, L0_attn_W_skip, L0_attn_b_skip,
           L0_attn_log_decay, L0_ffn_U1, L0_ffn_V1, L0_ffn_b1, L0_ffn_U2,
           L0_ffn_V2, L0_ffn_b2):
    f = np.float32
    tok = np.ascontiguousarray(np.asarray(L0_attn_tokens, f))
    rep = lambda v: np.ascontiguousarray(np.broadcast_to(np.asarray(v, f), (128, 1024)))
    chunked = lambda wmat, nch, cols: np.ascontiguousarray(
        np.asarray(wmat, f).reshape(nch, 128, cols).transpose(1, 0, 2))
    com = {
        "winT": chunked(np.asarray(L0_attn_W_in, f).T, 8, 256),
        "woutT": chunked(np.asarray(L0_attn_W_out, f).T, 2, 1024),
        "wskipT": chunked(np.asarray(L0_attn_W_skip, f).T, 8, 1024),
        "u1": chunked(L0_ffn_U1, 8, 256),
        "v1": chunked(L0_ffn_V1, 2, 4096),
        "u2": chunked(L0_ffn_U2, 32, 256),
        "v2": chunked(L0_ffn_V2, 2, 1024),
        "nwf": np.ascontiguousarray(np.asarray(L0_attn_norm_w, f).reshape(8, 128).T),
        "bo": rep(L0_attn_b_out),
        "bs": rep(L0_attn_b_skip),
        "b2": rep(L0_ffn_b2),
        "bin": np.ascontiguousarray(np.asarray(L0_attn_b_in, f).reshape(2, 128).T),
        "b1": np.ascontiguousarray(np.asarray(L0_ffn_b1, f).reshape(32, 128).T),
        "ldec": np.ascontiguousarray(np.asarray(L0_attn_log_decay, f).reshape(2, 128).T),
        "eye": np.eye(128, dtype=f),
    }
    in_maps = []
    for c in range(8):
        m = dict(com)
        m["x"] = np.ascontiguousarray(tok[2 * c:2 * c + 2])
        in_maps.append(m)
    nc = build_nc()
    res = run_bass_kernel_spmd(nc, in_maps, core_ids=list(range(8)))
    out = np.concatenate([res.results[c]["o"] for c in range(8)], axis=0)
    return np.ascontiguousarray(out.astype(np.float32))


# version 09
# speedup vs baseline: 1.0568x; 1.0568x over previous
import numpy as np
from contextlib import ExitStack

import concourse.bass as bass
import concourse.mybir as mybir
from concourse.bass_utils import run_bass_kernel_spmd

F32 = mybir.dt.float32
AF = mybir.ActivationFunctionType
OP = mybir.AluOpType

B_LOC = 2
S = 2048
D = 1024
TOK = 256
NTILE = B_LOC * S // TOK
ENGS = ("sync", "gpsimd", "vector", "scalar", "tensor")


def build_nc():
    nc = bass.Bass("TRN2", target_bir_lowering=False)
    x_h = nc.dram_tensor("x", [B_LOC, S, D], F32, kind="ExternalInput").ap()
    o_h = nc.dram_tensor("o", [B_LOC, S, D], F32, kind="ExternalOutput").ap()
    winT_h = nc.dram_tensor("winT", [128, 8, 256], F32, kind="ExternalInput").ap()
    woutT_h = nc.dram_tensor("woutT", [128, 2, 1024], F32, kind="ExternalInput").ap()
    wskipT_h = nc.dram_tensor("wskipT", [128, 8, 1024], F32, kind="ExternalInput").ap()
    u1_h = nc.dram_tensor("u1", [128, 8, 256], F32, kind="ExternalInput").ap()
    v1_h = nc.dram_tensor("v1", [128, 2, 4096], F32, kind="ExternalInput").ap()
    u2_h = nc.dram_tensor("u2", [128, 32, 256], F32, kind="ExternalInput").ap()
    v2_h = nc.dram_tensor("v2", [128, 2, 1024], F32, kind="ExternalInput").ap()
    nwf_h = nc.dram_tensor("nwf", [128, 8], F32, kind="ExternalInput").ap()
    bo_h = nc.dram_tensor("bo", [128, 1024], F32, kind="ExternalInput").ap()
    bs_h = nc.dram_tensor("bs", [128, 1024], F32, kind="ExternalInput").ap()
    b2_h = nc.dram_tensor("b2", [128, 1024], F32, kind="ExternalInput").ap()
    bin_h = nc.dram_tensor("bin", [128, 2], F32, kind="ExternalInput").ap()
    b1_h = nc.dram_tensor("b1", [128, 32], F32, kind="ExternalInput").ap()
    ldec_h = nc.dram_tensor("ldec", [128, 2], F32, kind="ExternalInput").ap()
    eye_h = nc.dram_tensor("eye", [128, 128], F32, kind="ExternalInput").ap()

    es = ExitStack()
    sb = lambda name, shape: es.enter_context(nc.sbuf_tensor(name, shape, F32))
    WIN = sb("WIN", [128, 8, 256])
    WOUT = sb("WOUT", [128, 2, 1024])
    WSKIP = sb("WSKIP", [128, 8, 1024])
    V1 = sb("V1", [128, 2, 4096])
    U2 = sb("U2", [128, 32, 256])
    V2 = sb("V2", [128, 2, 1024])
    BADD = sb("BADD", [128, 1024])
    B2R = sb("B2R", [128, 1024])
    BIN = sb("BIN", [128, 2])
    B1 = sb("B1", [128, 32])
    LDEC = sb("LDEC", [128, 2])
    NWF = sb("NWF", [128, 8])
    DECS = sb("DECS", [128, 2])
    EYE = sb("EYE", [128, 128])
    U = sb("U", [128, 2, 2 * S])
    XA = sb("XA", [128, 2048])
    SCR = sb("SCR", [128, 2048])
    T0 = sb("T0", [128, 2048])
    ATT = sb("ATT", [128, 2048])
    FFNB = sb("FFNB", [128, 2048])
    XNT = sb("XNT", [128, 4, 256])
    ST6 = sb("ST6", [128, 4, 6])
    ST2 = sb("ST2", [128, 2, 2])
    MSQ = sb("MSQ", [128, 2])
    MS = sb("MS", [128, 2])
    SD = sb("SD", [128, 2])
    RR = sb("RR", [128, 2])
    ZERO = sb("ZERO", [128, 1])
    EPS5 = sb("EPS5", [128, 1])
    EPS4 = sb("EPS4", [128, 1])
    PADT = sb("PADT", [128, 2])
    PADS = sb("PADS", [128, 2])

    ps = lambda name, shape: es.enter_context(nc.psum_tensor(name, shape, F32))
    PT0 = ps("PT0", [128, 4, 128])
    PT1 = ps("PT1", [128, 4, 128])
    PA = ps("PA", [128, 512])
    PB = ps("PB", [128, 512])
    PU = ps("PU", [128, 512])
    PP = ps("PP", [128, 512])
    PH = ps("PH", [128, 2, 256])
    PQ = ps("PQ", [128, 2, 256])

    sem_names = ("LX", "LW", "LW2", "VS", "SS", "TS", "HS", "OS")
    semh = {s: es.enter_context(nc.semaphore(s)) for s in sem_names}

    ops = {e: [] for e in ENGS}
    cnt = {s: 0 for s in sem_names}

    def w(eng, sem, val):
        if val > 0:
            ops[eng].append(lambda eh, sm=sem, v=val: eh.wait_ge(semh[sm], v))

    def e(eng, fn, inc=None, k=1):
        if inc is None:
            ops[eng].append(fn)
            return None
        cnt[inc] += k
        ops[eng].append(lambda eh, f=fn, s=inc, kk=k: f(eh).then_inc(semh[s], kk))
        return cnt[inc]

    def vpad(inc=None):
        return e("vector", lambda eh: eh.drain(), inc=inc)

    def spad(inc=None):
        return e("scalar", lambda eh: eh.drain(), inc=inc)

    wcnt = {}
    for name, dst, src in (
        ("eye", EYE[:], eye_h[:]), ("nw", NWF[:], nwf_h[:]), ("bin", BIN[:], bin_h[:]),
        ("b1", B1[:], b1_h[:]), ("ldec", LDEC[:], ldec_h[:]), ("bo", BADD[:], bo_h[:]),
        ("bs", SCR[:, 0:1024], bs_h[:]), ("b2", B2R[:], b2_h[:]),
        ("winT", WIN[:], winT_h[:]), ("woutT", WOUT[:], woutT_h[:]),
        ("wskipT", WSKIP[:], wskipT_h[:]),
    ):
        wcnt[name] = e("sync", lambda eh, d=dst, s=src: eh.dma_start(out=d, in_=s), inc="LW", k=16)
    w2cnt = {}
    for name, dst, src in (("v1", V1[:], v1_h[:]), ("u2", U2[:], u2_h[:]),
                           ("v2", V2[:], v2_h[:])):
        w2cnt[name] = e("scalar", lambda eh, d=dst, s=src: eh.dma_start(out=d, in_=s), inc="LW2", k=16)

    w("scalar", "LW", wcnt["ldec"])
    e("scalar", lambda eh: eh.activation(DECS[:], LDEC[:], AF.Sigmoid))
    ss_decs = spad(inc="SS")

    e("vector", lambda eh: eh.memset(ZERO[:], 0.0))
    e("vector", lambda eh: eh.memset(EPS5[:], 1e-5))
    e("vector", lambda eh: eh.memset(EPS4[:], 1e-4))
    e("vector", lambda eh: eh.memset(PADT[:], 0.0))
    w("vector", "LW", wcnt["bs"])
    e("vector", lambda eh: eh.tensor_tensor(BADD[:], BADD[:], SCR[:, 0:1024], OP.add))
    vs_badd = vpad(inc="VS")
    w("vector", "LW", wcnt["winT"])
    for c in range(8):
        e("vector", lambda eh, cc=c: eh.tensor_scalar_mul(
            WIN[:, cc, :], WIN[:, cc, :], NWF[:, cc:cc + 1]))
    vs_winfold = vpad(inc="VS")

    XB = [XA, SCR]
    xfree = [0] * (2 * NTILE + 2)

    def emit_xload(slot, vs_guard):
        i = slot % NTILE
        b, t = i // 8, (i % 8) * TOK
        buf = XB[slot % 2]
        if vs_guard:
            w("gpsimd", "VS", vs_guard)
        for blk in range(2):
            e("gpsimd", lambda eh, bb=b, tt=t + blk * 128, dst=buf[:, blk * 1024:(blk + 1) * 1024]:
              eh.dma_start(out=dst, in_=x_h[bb, tt:tt + 128, :]), inc="LX", k=16)

    def ln_stats(src):
        for blk in range(2):
            for g in range(2):
                e("vector", lambda eh, b=blk, gg=g, s=src:
                  eh.bn_stats(ST6[:, 2 * b + gg, :],
                              s[:, b * 1024 + gg * 512: b * 1024 + (gg + 1) * 512]))
        for blk in range(2):
            e("vector", lambda eh, b=blk: eh.bn_aggr(ST2[:, b, :], ST6[:, 2 * b:2 * b + 2, :]))

    def sqrt_recip(src_col_fn, eps):
        for blk in range(2):
            e("scalar", lambda eh, b=blk: eh.activation(
                SD[:, b:b + 1], src_col_fn(b), AF.Sqrt, bias=eps[:, 0:1]))
        ss = spad(inc="SS")
        w("vector", "SS", ss)
        for blk in range(2):
            e("vector", lambda eh, b=blk: eh.reciprocal(RR[:, b:b + 1], SD[:, b:b + 1]))
        e("vector", lambda eh: eh.drain())

    def emit_tp_group(src, g):
        for blk in range(2):
            PT = (PT0, PT1)[blk]
            for j in range(4):
                c = 4 * g + j
                e("tensor", lambda eh, bb=blk, cc=c, jj=j, P=PT, s=src: eh.transpose(
                    P[:, jj, :], s[:, bb * 1024 + cc * 128: bb * 1024 + (cc + 1) * 128], EYE[:]),
                  inc="TS" if j == 3 else None)
            w("vector", "TS", cnt["TS"])
            e("vector", lambda eh, bb=blk, P=PT: eh.tensor_copy(
                XNT[:, :, bb * 128:(bb + 1) * 128], P[:, :, :]))
            ev = vpad(inc="VS")
            w("tensor", "VS", ev)

    def vfront(slot, xn_dst, tp_guard_ts=0):
        buf = XB[slot % 2]
        w("vector", "LX", 32 * (slot + 1))
        vs_st = None
        ln_stats(buf)
        vs_st = vpad(inc="VS")
        w("scalar", "VS", vs_st)
        sqrt_recip(lambda blk: ST2[:, blk, 1:2], EPS5)
        for blk in range(2):
            e("vector", lambda eh, bb=blk, s=buf: eh.tensor_scalar(
                T0[:, bb * 1024:(bb + 1) * 1024], s[:, bb * 1024:(bb + 1) * 1024],
                ST2[:, bb, 0:1], RR[:, bb:bb + 1], OP.subtract, OP.mult))
        ln_stats(T0)
        for blk in range(2):
            e("vector", lambda eh, bb=blk: eh.tensor_tensor(
                MSQ[:, bb:bb + 1], ST2[:, bb, 0:1], ST2[:, bb, 0:1], OP.mult))
        for blk in range(2):
            e("vector", lambda eh, bb=blk: eh.tensor_tensor(
                MS[:, bb:bb + 1], MSQ[:, bb:bb + 1], ST2[:, bb, 1:2], OP.add))
        vs_st2 = vpad(inc="VS")
        w("scalar", "VS", vs_st2)
        sqrt_recip(lambda blk: MS[:, blk:blk + 1], EPS4)
        if tp_guard_ts:
            w("vector", "TS", tp_guard_ts)
        for blk in range(2):
            e("vector", lambda eh, bb=blk, d=xn_dst: eh.tensor_scalar_mul(
                d[:, bb * 1024:(bb + 1) * 1024], T0[:, bb * 1024:(bb + 1) * 1024],
                RR[:, bb:bb + 1]))
        return vpad(inc="VS")

    vs_uevac = [0] * NTILE
    vs_xn1 = [0] * NTILE
    ts_tp1 = [0] * NTILE

    emit_xload(0, 0)
    emit_xload(1, vs_badd)
    xnb = lambda i: (ATT, FFNB)[i % 2]
    vs_xn1[0] = vfront(0, xnb(0))
    xfree[0] = vs_xn1[0]
    emit_xload(2, xfree[0])
    vs_xn1[1] = vfront(1, xnb(1))
    xfree[1] = vs_xn1[1]
    emit_xload(3, xfree[1])

    for i in range(NTILE):
        b, t = i // 8, (i % 8) * TOK
        if i == 0:
            w("tensor", "VS", vs_winfold)
        w("tensor", "VS", vs_xn1[i])
        if i >= 1:
            w("tensor", "VS", vs_uevac[i - 1])
        for g in range(2):
            emit_tp_group(xnb(i), g)
            for nt in range(2):
                PO = (PU, PA)[nt]
                for c in range(4 * g, 4 * g + 4):
                    e("tensor", lambda eh, n=nt, cc=c, P=PO: eh.matmul(
                        P[:, 0:256], WIN[:, cc, n * 128:(n + 1) * 128],
                        XNT[:, cc % 4, :], start=(cc == 0), stop=(cc == 7)),
                      inc="TS" if (c == 7 and nt == 1) else None)
        ts_tp1[i] = cnt["TS"]
        w("vector", "TS", ts_tp1[i])
        if i == 0:
            w("vector", "LW", wcnt["bin"])
        for nt in range(2):
            PO = (PU, PA)[nt]
            e("vector", lambda eh, n=nt, bb=b, tt=t, P=PO: eh.tensor_scalar_add(
                U[:, n, bb * S + tt: bb * S + tt + TOK], P[:, 0:256],
                BIN[:, n:n + 1]))
        vs_uevac[i] = vpad(inc="VS")
        if i + 2 < NTILE:
            vs_xn1[i + 2] = vfront(i + 2, xnb(i + 2), tp_guard_ts=ts_tp1[i])
            xfree[i + 2] = vs_xn1[i + 2]
            if i + 4 < NTILE:
                emit_xload(i + 4, xfree[i + 2])

    w("vector", "LW", wcnt["wskipT"])
    for c in range(8):
        e("vector", lambda eh, cc=c: eh.tensor_scalar_mul(
            WSKIP[:, cc, :], WSKIP[:, cc, :], NWF[:, cc:cc + 1]))
    vpad()
    w("scalar", "TS", ts_tp1[NTILE - 1])
    w2cnt["u1"] = e("scalar", lambda eh: eh.dma_start(out=WIN[:], in_=u1_h[:]), inc="LW2", k=16)

    w("vector", "SS", ss_decs)
    for nt in range(2):
        e("vector", lambda eh: eh.memset(SCR[:], 1.0))
        vpad()
        e("vector", lambda eh, n=nt: eh.tensor_scalar_mul(SCR[:], SCR[:], DECS[:, n:n + 1]))
        vpad()
        for bb in range(2):
            e("vector", lambda eh, n=nt, b2=bb: eh.tensor_tensor_scan(
                U[:, n, b2 * S:(b2 + 1) * S], SCR[:], U[:, n, b2 * S:(b2 + 1) * S],
                ZERO[:, 0:1], OP.mult, OP.add))
    vs_scan = vpad(inc="VS")
    w("tensor", "VS", vs_scan)
    emit_xload(NTILE + 0, xfree[14])

    os_done = [0] * NTILE

    def vfront2(i):
        slot = NTILE + i
        buf = XB[i % 2]
        w("vector", "LX", 32 * (slot + 1))
        ln_stats(buf)
        vs_st = vpad(inc="VS")
        w("scalar", "VS", vs_st)
        sqrt_recip(lambda blk: ST2[:, blk, 1:2], EPS5)
        for blk in range(2):
            e("vector", lambda eh, bb=blk, s=buf: eh.tensor_scalar(
                T0[:, bb * 1024:(bb + 1) * 1024], s[:, bb * 1024:(bb + 1) * 1024],
                ST2[:, bb, 0:1], RR[:, bb:bb + 1], OP.subtract, OP.mult))
        ln_stats(T0)
        for blk in range(2):
            e("vector", lambda eh, bb=blk: eh.tensor_tensor(
                MSQ[:, bb:bb + 1], ST2[:, bb, 0:1], ST2[:, bb, 0:1], OP.mult))
        for blk in range(2):
            e("vector", lambda eh, bb=blk: eh.tensor_tensor(
                MS[:, bb:bb + 1], MSQ[:, bb:bb + 1], ST2[:, bb, 1:2], OP.add))
        vs_st2 = vpad(inc="VS")
        w("scalar", "VS", vs_st2)
        sqrt_recip(lambda blk: MS[:, blk:blk + 1], EPS4)
        for blk in range(2):
            e("vector", lambda eh, bb=blk: eh.tensor_scalar_mul(
                ATT[:, bb * 1024:(bb + 1) * 1024], T0[:, bb * 1024:(bb + 1) * 1024],
                RR[:, bb:bb + 1]))
        return vpad(inc="VS")

    vs_xn2 = [0] * NTILE
    vs_xn2[0] = vfront2(0)
    emit_xload(NTILE + 1, vs_scan)

    COMBO = [(0, 0, PA), (0, 1, PB), (1, 0, PU), (1, 1, PP)]
    for i in range(NTILE):
        b, t = i // 8, (i % 8) * TOK
        buf = XB[i % 2]
        if 1 <= i and i + 1 < NTILE:
            emit_xload(NTILE + i + 1, xfree[NTILE + i - 1])

        w("tensor", "VS", vs_xn2[i])
        for g in range(2):
            emit_tp_group(ATT, g)
            if g == 0:
                for blk, half, PO in COMBO:
                    for nt in range(2):
                        e("tensor", lambda eh, n=nt, bb=b, ttk=t + blk * 128, hh=half, P=PO: eh.matmul(
                            P[:], U[:, n, bb * S + ttk: bb * S + ttk + 128],
                            WOUT[:, n, hh * 512:(hh + 1) * 512],
                            start=(n == 0), stop=False))
            for blk, half, PO in COMBO:
                for c in range(4 * g, 4 * g + 4):
                    e("tensor", lambda eh, cc=c, bb=blk, hh=half, P=PO: eh.matmul(
                        P[:], XNT[:, cc % 4, bb * 128:(bb + 1) * 128],
                        WSKIP[:, cc, hh * 512:(hh + 1) * 512],
                        start=False, stop=(cc == 7)),
                      inc="TS" if (c == 7 and blk == 1 and half == 1) else None)
        ts_y = cnt["TS"]

        w("vector", "TS", ts_y)
        if i == 0:
            w("vector", "LW", wcnt["b2"])
        for blk, half, PO in COMBO:
            e("vector", lambda eh, bb=blk, hh=half, P=PO: eh.tensor_tensor(
                ATT[:, bb * 1024 + hh * 512: bb * 1024 + (hh + 1) * 512], P[:],
                T0[:, bb * 1024 + hh * 512: bb * 1024 + (hh + 1) * 512], OP.add))
        for blk, half, PO in COMBO:
            e("vector", lambda eh, bb=blk, hh=half: eh.tensor_tensor(
                ATT[:, bb * 1024 + hh * 512: bb * 1024 + (hh + 1) * 512],
                ATT[:, bb * 1024 + hh * 512: bb * 1024 + (hh + 1) * 512],
                BADD[:, hh * 512:(hh + 1) * 512], OP.add))
        ev = vpad(inc="VS")
        w("tensor", "VS", ev)
        ln_stats(ATT)
        vs_st = vpad(inc="VS")
        w("scalar", "VS", vs_st)
        sqrt_recip(lambda blk: ST2[:, blk, 1:2], EPS5)
        for blk in range(2):
            e("vector", lambda eh, bb=blk: eh.tensor_scalar(
                T0[:, bb * 1024:(bb + 1) * 1024], ATT[:, bb * 1024:(bb + 1) * 1024],
                ST2[:, bb, 0:1], RR[:, bb:bb + 1], OP.subtract, OP.mult))
        for blk in range(2):
            e("vector", lambda eh, bb=blk, s=buf: eh.tensor_tensor(
                s[:, bb * 1024:(bb + 1) * 1024], s[:, bb * 1024:(bb + 1) * 1024],
                T0[:, bb * 1024:(bb + 1) * 1024], OP.add))
        ln_stats(buf)
        vs_st = vpad(inc="VS")
        w("scalar", "VS", vs_st)
        sqrt_recip(lambda blk: ST2[:, blk, 1:2], EPS5)
        for blk in range(2):
            e("vector", lambda eh, bb=blk, s=buf: eh.tensor_scalar(
                T0[:, bb * 1024:(bb + 1) * 1024], s[:, bb * 1024:(bb + 1) * 1024],
                ST2[:, bb, 0:1], RR[:, bb:bb + 1], OP.subtract, OP.mult))
        vs_t1 = vpad(inc="VS")

        w("tensor", "VS", vs_t1)
        if i == 0:
            w("tensor", "LW2", w2cnt["u1"])
        for g in range(2):
            emit_tp_group(T0, g)
            for r in range(2):
                PO = (PU, PA)[r]
                for c in range(4 * g, 4 * g + 4):
                    e("tensor", lambda eh, rr=r, cc=c, P=PO: eh.matmul(
                        P[:, 0:256], WIN[:, cc, rr * 128:(rr + 1) * 128],
                        XNT[:, cc % 4, :], start=(cc == 0), stop=(cc == 7)),
                      inc="TS" if (c == 7 and r == 1) else None)
        w("vector", "TS", cnt["TS"])
        if i >= 1:
            w("vector", "OS", os_done[i - 1])
        for r in range(2):
            PO = (PU, PA)[r]
            e("vector", lambda eh, rr=r, P=PO: eh.tensor_copy(
                FFNB[:, rr * 256:(rr + 1) * 256], P[:, 0:256]))
        vs_p = vpad(inc="VS")
        w("tensor", "VS", vs_p)

        hs_base = cnt["HS"]
        for c in range(33):
            if c < 32:
                if c >= 2:
                    w("tensor", "HS", hs_base + c - 1)
                for r in range(2):
                    e("tensor", lambda eh, rr=r, cc=c: eh.matmul(
                        PH[:, cc % 2, :], V1[:, rr, cc * 128:(cc + 1) * 128],
                        FFNB[:, rr * 256:(rr + 1) * 256],
                        start=(rr == 0), stop=(rr == 1)),
                      inc="TS" if r == 1 else None)
                w("scalar", "TS", cnt["TS"])
                if i == 0 and c == 0:
                    w("scalar", "LW", wcnt["b1"])
                e("scalar", lambda eh, cc=c: eh.activation(
                    FFNB[:, 1536 + (cc % 2) * 256: 1536 + (cc % 2 + 1) * 256],
                    PH[:, cc % 2, :], AF.Gelu_apprx_tanh, bias=B1[:, cc:cc + 1]))
                spad(inc="HS")
            if c >= 1:
                cc0 = c - 1
                w("tensor", "HS", hs_base + cc0 + 1)
                for r in range(2):
                    PO = PQ[:, 0, :] if r == 0 else PB[:, 0:256]
                    e("tensor", lambda eh, rr=r, cc=cc0, P=PO: eh.matmul(
                        P, U2[:, cc, rr * 128:(rr + 1) * 128],
                        FFNB[:, 1536 + (cc % 2) * 256: 1536 + (cc % 2 + 1) * 256],
                        start=(cc == 0), stop=(cc == 31)),
                      inc="TS" if (r == 1 and cc0 == 31) else None)
        ts_q = cnt["TS"]

        if i + 1 < NTILE:
            vs_xn2[i + 1] = vfront2(i + 1)

        w("vector", "TS", ts_q)
        for r in range(2):
            PO = PQ[:, 0, :] if r == 0 else PB[:, 0:256]
            e("vector", lambda eh, rr=r, P=PO: eh.tensor_copy(
                FFNB[:, 1024 + rr * 256:1024 + (rr + 1) * 256], P))
        vs_q = vpad(inc="VS")
        w("tensor", "VS", vs_q)
        for blk in range(2):
            for half in range(2):
                PO = (PA, PB)[half]
                for r in range(2):
                    e("tensor", lambda eh, rr=r, bb=blk, hh=half, P=PO: eh.matmul(
                        P[:], FFNB[:, 1024 + rr * 256 + bb * 128: 1024 + rr * 256 + (bb + 1) * 128],
                        V2[:, rr, hh * 512:(hh + 1) * 512],
                        start=(rr == 0), stop=(rr == 1)),
                      inc="TS" if r == 1 else None)
            w("vector", "TS", cnt["TS"])
            for half in range(2):
                PO = (PA, PB)[half]
                e("vector", lambda eh, bb=blk, hh=half, P=PO: eh.tensor_tensor(
                    FFNB[:, bb * 1024 + hh * 512: bb * 1024 + (hh + 1) * 512], P[:],
                    B2R[:, hh * 512:(hh + 1) * 512], OP.add))
            evb = vpad(inc="VS")
            w("tensor", "VS", evb)
        ln_stats(FFNB)
        vs_st = vpad(inc="VS")
        w("scalar", "VS", vs_st)
        sqrt_recip(lambda blk: ST2[:, blk, 1:2], EPS5)
        for blk in range(2):
            e("vector", lambda eh, bb=blk: eh.tensor_scalar(
                FFNB[:, bb * 1024:(bb + 1) * 1024], FFNB[:, bb * 1024:(bb + 1) * 1024],
                ST2[:, bb, 0:1], RR[:, bb:bb + 1], OP.subtract, OP.mult))
        for blk in range(2):
            e("vector", lambda eh, bb=blk, s=buf: eh.tensor_tensor(
                FFNB[:, bb * 1024:(bb + 1) * 1024], FFNB[:, bb * 1024:(bb + 1) * 1024],
                s[:, bb * 1024:(bb + 1) * 1024], OP.add))
        evf = vpad(inc="VS")
        xfree[NTILE + i] = evf
        w("gpsimd", "VS", evf)
        for blk in range(2):
            e("gpsimd", lambda eh, bb=b, ttk=t + blk * 128, o=blk: eh.dma_start(
                out=o_h[bb, ttk:ttk + 128, :], in_=FFNB[:, o * 1024:(o + 1) * 1024]),
              inc="OS", k=16)
        os_done[i] = cnt["OS"]

    w("gpsimd", "OS", cnt["OS"])

    with es, nc.Block() as block:
        @block.sync
        def _(sync):
            for f in ops["sync"]:
                f(sync)

        @block.gpsimd
        def _(gpsimd):
            for f in ops["gpsimd"]:
                f(gpsimd)

        @block.vector
        def _(vector):
            for f in ops["vector"]:
                f(vector)

        @block.scalar
        def _(scalar):
            for f in ops["scalar"]:
                f(scalar)

        @block.tensor
        def _(tensor):
            for f in ops["tensor"]:
                f(tensor)
    return nc


def kernel(L0_attn_tokens, L0_attn_norm_w, L0_attn_W_in, L0_attn_b_in,
           L0_attn_W_out, L0_attn_b_out, L0_attn_W_skip, L0_attn_b_skip,
           L0_attn_log_decay, L0_ffn_U1, L0_ffn_V1, L0_ffn_b1, L0_ffn_U2,
           L0_ffn_V2, L0_ffn_b2):
    f = np.float32
    tok = np.ascontiguousarray(np.asarray(L0_attn_tokens, f))
    rep = lambda v: np.ascontiguousarray(np.broadcast_to(np.asarray(v, f), (128, 1024)))
    chunked = lambda wmat, nch, cols: np.ascontiguousarray(
        np.asarray(wmat, f).reshape(nch, 128, cols).transpose(1, 0, 2))
    com = {
        "winT": chunked(np.asarray(L0_attn_W_in, f).T, 8, 256),
        "woutT": chunked(np.asarray(L0_attn_W_out, f).T, 2, 1024),
        "wskipT": chunked(np.asarray(L0_attn_W_skip, f).T, 8, 1024),
        "u1": chunked(L0_ffn_U1, 8, 256),
        "v1": chunked(L0_ffn_V1, 2, 4096),
        "u2": chunked(L0_ffn_U2, 32, 256),
        "v2": chunked(L0_ffn_V2, 2, 1024),
        "nwf": np.ascontiguousarray(np.asarray(L0_attn_norm_w, f).reshape(8, 128).T),
        "bo": rep(L0_attn_b_out),
        "bs": rep(L0_attn_b_skip),
        "b2": rep(L0_ffn_b2),
        "bin": np.ascontiguousarray(np.asarray(L0_attn_b_in, f).reshape(2, 128).T),
        "b1": np.ascontiguousarray(np.asarray(L0_ffn_b1, f).reshape(32, 128).T),
        "ldec": np.ascontiguousarray(np.asarray(L0_attn_log_decay, f).reshape(2, 128).T),
        "eye": np.eye(128, dtype=f),
    }
    in_maps = []
    for c in range(8):
        m = dict(com)
        m["x"] = np.ascontiguousarray(tok[2 * c:2 * c + 2])
        in_maps.append(m)
    nc = build_nc()
    res = run_bass_kernel_spmd(nc, in_maps, core_ids=list(range(8)))
    out = np.concatenate([res.results[c]["o"] for c in range(8)], axis=0)
    return np.ascontiguousarray(out.astype(np.float32))
